# version 01
# baseline (speedup 1.0000x reference)
import os
import math
import contextlib
import functools
import numpy as np
import ml_dtypes
import concourse.bass as bass
import concourse.mybir as mybir
from concourse.bass import AP
from concourse.bass_utils import run_bass_kernel_spmd

F32 = mybir.dt.float32
BF16 = mybir.dt.bfloat16
AF = mybir.ActivationFunctionType
ALU = mybir.AluOpType
AXX = mybir.AxisListType.X

T = 2048
D = 1024
DC = D // 128
NH = 16
KVH = 4
HD = 64
LAT = 256
WIN = 256
I_FF = 2048
NE = 16
TOPK = 4
EPS_LN = 1e-5
EPS_RMS = 1e-6

ENGS = ("sync", "tensor", "vector", "scalar", "gpsimd")
PAIRS = (0, 1, 2, 3, 8, 9, 10, 11)


def _host_tables():
    import jax

    def rope_tables(base):
        with jax.default_device(jax.devices("cpu")[0]):
            import jax.numpy as jnp
            inv_freq = 1.0 / base ** (jnp.arange(0, HD, 2, dtype=jnp.float32) / HD)
            freqs = jnp.arange(T, dtype=jnp.float32)[:, None] * inv_freq[None, :]
            cos = np.asarray(jnp.cos(freqs))
            sin = np.asarray(jnp.sin(freqs))
        return cos, sin

    out = {}
    for tag, base in (("g", 1000000.0), ("l", 10000.0)):
        cos, sin = rope_tables(base)
        cos2 = np.empty((128, T), np.float32)
        sinS = np.empty((128, T), np.float32)
        for d in range(128):
            i = (d % HD) // 2
            cos2[d] = cos[:, i]
            sinS[d] = (-sin[:, i]) if (d % 2 == 0) else sin[:, i]
        out[f"cos_{tag}"] = cos2
        out[f"sin_{tag}"] = sinS

    out["ident"] = np.eye(128, dtype=np.float32)
    out["ident_b"] = np.eye(128, dtype=ml_dtypes.bfloat16)
    tri_c = np.zeros((128, 128), np.float32)
    for k in range(128):
        tri_c[k, k:] = 1.0
    out["tri_c"] = tri_c.astype(ml_dtypes.bfloat16)
    tri_w = np.zeros((128, 128), np.float32)
    for k in range(128):
        tri_w[k, :k] = 1.0
    out["tri_w"] = tri_w.astype(ml_dtypes.bfloat16)

    out["epsc"] = np.tile(np.array([[EPS_LN, EPS_RMS, 0.0, 0.0]], np.float32),
                          (128, 1))
    out["ones_col_b"] = np.ones((128, 1), ml_dtypes.bfloat16)
    out["ones128_b"] = np.ones((128, 128), ml_dtypes.bfloat16)
    smu = np.zeros((128, 2), np.float32); smu[:, 0] = 1.0 / D
    ssg = np.zeros((128, 2), np.float32); ssg[:, 1] = 1.0 / D
    out["smu_b"] = smu.astype(ml_dtypes.bfloat16)
    out["ssg_b"] = ssg.astype(ml_dtypes.bfloat16)
    s2 = np.zeros((2, 256), np.float32)
    s2[0, 0:128] = 1.0; s2[1, 128:256] = 1.0
    out["sel2_f"] = s2
    s16 = np.zeros((16, 16 * 128), np.float32)
    for i in range(16):
        s16[i, i * 128:(i + 1) * 128] = 1.0
    out["sel16_b"] = s16.astype(ml_dtypes.bfloat16)
    sp = np.zeros((16, 8 * 128), np.float32)
    for j in range(8):
        sp[2 * j, j * 128:j * 128 + 64] = 1.0
        sp[2 * j + 1, j * 128 + 64:(j + 1) * 128] = 1.0
    out["selp_b"] = sp.astype(ml_dtypes.bfloat16)
    out["ones_row_b"] = np.ones((1, 128), ml_dtypes.bfloat16)
    out["ones_row_f"] = np.ones((1, 128), np.float32)
    out["neg_big"] = np.full((128, 256), -1e30, np.float32)
    hq2 = np.zeros((128, 2), np.float32)
    hq2[0:64, 0] = 1.0 / HD
    hq2[64:128, 1] = 1.0 / HD
    out["hq2_b"] = hq2.astype(ml_dtypes.bfloat16)
    vinit = np.zeros((128, 16 * KVH * 65), np.float32)
    for blk in range(16 * KVH):
        vinit[:, blk * 65 + 64] = 1.0
    out["vinit"] = vinit.astype(ml_dtypes.bfloat16)
    return out


class Build:
    def __init__(self):
        self.q = {e: [] for e in ENGS}
        self.cnt = {}
        self.watermark = {}
        self.tbl = {}
        self.hbm = {}
        self.sem_names = set()

    def on(self, eng, fn):
        self.q[eng].append(fn)

    def wait(self, eng, sem, val):
        if val <= 0:
            return
        key = (eng, sem)
        if self.watermark.get(key, -1) >= val:
            return
        self.watermark[key] = val
        self.on(eng, lambda e, s=sem, v=val: e.wait_ge(self._sems[s], v))

    def _bump(self, sem, amt):
        self.sem_names.add(sem)
        self.cnt[sem] = self.cnt.get(sem, 0) + amt
        return self.cnt[sem]

    def op(self, eng, fn, inc=None):
        if inc is None:
            self.on(eng, fn)
            return None
        sem, amt = (inc, 1) if isinstance(inc, str) else inc
        val = self._bump(sem, amt)
        def run(e, fn=fn, s=sem, a=amt):
            ins = fn(e)
            ins.then_inc(self._sems[s], a)
        self.on(eng, run)
        return val

    def dma(self, eng, out, in_, sem="cst", slow=False):
        return self.op(eng, lambda e, o=out, i=in_, sl=slow: e.dma_start(
            out=o, in_=i, allow_slow_non_contiguous=sl), inc=(sem, 16))

    def drain(self, eng, inc=None):
        return self.op(eng, lambda e: e.drain(), inc=inc)

    def barrier(self, engs=ENGS):
        vals = {}
        for e in engs:
            vals[e] = self.op(e, lambda x: x.drain(), inc=("bar_" + e, 1))
        for e in engs:
            for o in engs:
                if o != e:
                    self.wait(e, "bar_" + o, vals[o])

    def realize(self, nc, es):
        sems = {}
        for s in sorted(self.sem_names):
            sems[s] = es.enter_context(nc.semaphore(s))
        self._sems = sems
        with_block = nc.Block()
        with with_block as block:
            for eng in ENGS:
                ops = self.q[eng]
                def body(e, ops=ops):
                    for fn in ops:
                        fn(e)
                getattr(block, eng)(body)


def _slc(h, p0, p1, c0, c1):
    return h[p0:p1, c0:c1]


DBG = []


def build_program(debug=()):
    nc = bass.Bass("TRN2", target_bir_lowering=False)
    b = Build()

    def din(name, shape, dt=F32):
        b.hbm[name] = nc.dram_tensor(name, list(shape), dt, kind="ExternalInput").ap()
        return b.hbm[name]

    xin = din("xin", [T, D])
    for w, sh in (("g_wq", [D, D]), ("g_kv", [D, LAT]), ("g_ku", [LAT, LAT]),
                  ("g_vu", [LAT, LAT]), ("g_wo", [D, D]), ("l_wq", [D, D]),
                  ("l_wk", [D, LAT]), ("l_wv", [D, LAT]), ("l_wo", [D, D]),
                  ("rw", [D, NE]), ("rb", [NE]), ("n1w", [D]), ("n2w", [D]),
                  ("w1", [NE, D, I_FF]), ("b1", [NE, I_FF]),
                  ("w2", [NE, I_FF, D]), ("b2", [NE, D])):
        din(w, sh)
    for cname, arr_dt in (("cos_g", F32), ("sin_g", F32), ("cos_l", F32),
                          ("sin_l", F32), ("ident", F32), ("ident_b", BF16),
                          ("tri_c", BF16),
                          ("tri_w", BF16), ("ones_col_b", BF16),
                          ("ones_row_b", BF16), ("ones_row_f", F32),
                          ("neg_big", F32), ("hq2_b", BF16),
                          ("vinit", BF16), ("ones128_b", BF16),
                          ("smu_b", BF16), ("ssg_b", BF16), ("sel2_f", F32),
                          ("sel16_b", BF16), ("selp_b", BF16), ("epsc", F32)):
        shp = {"cos_g": [128, T], "sin_g": [128, T], "cos_l": [128, T],
               "sin_l": [128, T], "ident": [128, 128], "ident_b": [128, 128],
               "tri_c": [128, 128],
               "tri_w": [128, 128], "ones_col_b": [128, 1],
               "ones_row_b": [1, 128], "ones_row_f": [1, 128],
               "neg_big": [128, 256], "hq2_b": [128, 2],
               "vinit": [128, 16 * KVH * 65],
               "ones128_b": [128, 128], "smu_b": [128, 2], "ssg_b": [128, 2],
               "sel2_f": [2, 256], "sel16_b": [16, 16 * 128],
               "selp_b": [16, 8 * 128], "epsc": [128, 4]}[cname]
        din(cname, shp, arr_dt)

    yout = nc.dram_tensor("yout", [T, D], F32, kind="ExternalOutput").ap()
    b.hbm["yout"] = yout
    dbg_out = {}
    for name in debug:
        shp = {"X": [128, DC * T], "U": [128, DC * T], "STAT": [32, T]}.get(
            name, [128, DC * T])
        dbg_out[name] = nc.dram_tensor("dbg_" + name, shp, F32,
                                       kind="ExternalOutput").ap()

    es = contextlib.ExitStack()
    E = es.enter_context

    tb = b.tbl
    tb["X"] = E(nc.sbuf_tensor("X", [128, DC * T], F32))
    tb["U"] = E(nc.sbuf_tensor("U", [128, DC * T], BF16))
    tb["UN"] = E(nc.sbuf_tensor("UN", [128, 32832], BF16))
    tb["CS"] = E(nc.sbuf_tensor("CS", [128, 2 * T], F32))
    tb["SCR"] = E(nc.sbuf_tensor("SCR", [128, T], F32))
    tb["LROW"] = E(nc.sbuf_tensor("LROW", [2, T], F32))
    tb["TOKS"] = E(nc.sbuf_tensor("TOKS", [128, 256], F32))
    tb["SMU"] = E(nc.sbuf_tensor("SMU", [128, 2], BF16))
    tb["SSG"] = E(nc.sbuf_tensor("SSG", [128, 2], BF16))
    tb["SEL2"] = E(nc.sbuf_tensor("SEL2", [2, 256], F32))
    tb["SEL16"] = E(nc.sbuf_tensor("SEL16", [16, 16 * 128], BF16))
    tb["SELP"] = E(nc.sbuf_tensor("SELP", [16, 8 * 128], BF16))
    tb["ONES128"] = E(nc.sbuf_tensor("ONES128", [128, 128], BF16))
    tb["EPSC"] = E(nc.sbuf_tensor("EPSC", [128, 4], F32))
    tb["IDY"] = E(nc.sbuf_tensor("IDY", [128, 128], F32))
    tb["IDB"] = E(nc.sbuf_tensor("IDB", [128, 128], BF16))
    tb["TRI"] = E(nc.sbuf_tensor("TRI", [128, 256], BF16))
    tb["HQ2"] = E(nc.sbuf_tensor("HQ2", [128, 2], BF16))
    tb["N1C"] = E(nc.sbuf_tensor("N1C", [128, DC], F32))
    tb["N2C"] = E(nc.sbuf_tensor("N2C", [128, DC], F32))
    tb["RBC"] = E(nc.sbuf_tensor("RBC", [NE, 1], F32))
    tb["B1T"] = E(nc.sbuf_tensor("B1T", [128, NE * NE], F32))
    tb["RMSK"] = E(nc.sbuf_tensor("RMSK", [128, 256], mybir.dt.uint8))
    tb["STATB"] = E(nc.sbuf_tensor("STATB", [16, T], BF16))

    tb["PA"] = E(nc.psum_tensor("PA", [128, T], F32))
    tb["PB"] = E(nc.psum_tensor("PB", [128, T], F32))

    UNMAP_ATT = {"KN": (0, 4096), "VT": (4096, 8256), "WW": (8256, 19520),
                 "QN": (19520, 21568), "MERG": (21568, 29760),
                 "PT": (29760, 30784), "BCB": (30784, 32832)}
    UNMAP_FFN = {"W1R": (0, 8192), "W2R": (8192, 12288), "CEB": (12288, 14336),
                 "H": (14336, 30720), "FB": (30720, 32832)}

    def un(tag, phase="att"):
        m = UNMAP_ATT if phase == "att" else UNMAP_FFN
        lo, hi = m[tag]
        return tb["UN"][:, lo:hi]

    nc.const_aps.register(F32, EPS_LN, tb["EPSC"][:, 0:1])
    nc.const_aps.register(F32, EPS_RMS, tb["EPSC"][:, 1:2])

    emit_phases(b, tb, un, debug, dbg_out)

    with nc.allow_low_precision("bf16 matmul operands / fp32 accumulation "
                                "(sanctioned for this problem)"):
        b.realize(nc, es)
    es.close()
    return nc


def emit_phases(b, tb, un, debug, dbg_out):
    X, U = tb["X"], tb["U"]
    PA, PB = tb["PA"], tb["PB"]
    hbm = b.hbm

    c_loads = []
    for nm, dst in (("ident", "IDY"), ("ident_b", "IDB"),
                    ("hq2_b", "HQ2"),
                    ("ones128_b", "ONES128"),
                    ("smu_b", "SMU"), ("ssg_b", "SSG"), ("sel2_f", "SEL2"),
                    ("sel16_b", "SEL16"), ("selp_b", "SELP"),
                    ("epsc", "EPSC")):
        c_loads.append(b.dma("sync", tb[dst][:, :], hbm[nm]))
    b.dma("sync", tb["TRI"][:, 0:128], hbm["tri_c"])
    tri_done = b.dma("sync", tb["TRI"][:, 128:256], hbm["tri_w"])
    n_done = None
    for nm, dst in (("n1w", "N1C"), ("n2w", "N2C")):
        src = AP(hbm[nm].tensor, 0, [[1, 128], [128, DC]])
        n_done = b.dma("sync", tb[dst][:, :], src, slow=True)
    const_done = b.cnt["cst"]

    id_done = const_done
    tr_done = {}
    for tt in range(T // 128):
        par = tt % 2
        stg = tb["SCR"][:, par * D:(par + 1) * D]
        if tt >= 2:
            b.wait("sync", "pe", tr_done[tt - 2])
        ld = b.dma("sync", stg, hbm["xin"][tt * 128:(tt + 1) * 128, :],
                   sem="ldx%d" % par)
        b.wait("tensor", "cst", const_done)
        b.wait("tensor", "ldx%d" % par, ld)
        ps = [PA, PB][par]
        pe_done = None
        for dc in range(DC):
            pe_done = b.op("tensor", lambda e, s=stg, dc=dc, ps=ps: e.transpose(
                ps[:, dc * 128:(dc + 1) * 128], s[:, dc * 128:(dc + 1) * 128],
                tb["IDY"][:, :]), inc="pe" if dc == DC - 1 else None)
        tr_done[tt] = pe_done
        b.wait("vector", "pe", pe_done)
        for half in range(2):
            src = AP(ps, half * 512,
                     [[T, 128], [128, 4], [1, 128]])
            dst = AP(X, (half * 4) * T + tt * 128,
                     [[DC * T, 128], [T, 4], [1, 128]])
            b.op("vector", lambda e, s=src, d=dst: e.tensor_copy(d, s),
                 inc="v" if half == 1 else None)
        b.wait("tensor", "v", b.cnt["v"])

    x_ready_v = b.cnt["v"]
    b.barrier()

    if "Xin" in debug:
        tap_x(b, tb, dbg_out["Xin"])
        b.barrier()

    ph_ln1(b, tb, un)
    if "X" in debug:
        b.barrier(); tap_t(b, tb["X"], dbg_out["X"], "dbgX")
    if "U" in debug:
        b.barrier(); tap_t(b, tb["U"], dbg_out["U"], "dbgU")

    PHASES = int(os.environ.get("KPHASES", "9"))
    if PHASES >= 2:
        b.barrier()
        ph_attn(b, tb, un, "g")
        if "Xg" in debug:
            b.barrier(); tap_t(b, tb["X"], dbg_out["Xg"], "dbgXg")
        if "Ug" in debug:
            b.barrier(); tap_t(b, tb["U"], dbg_out["Ug"], "dbgUg")
    if PHASES >= 3:
        b.barrier()
        ph_mid(b, tb, un)
        b.barrier()
        ph_attn(b, tb, un, "l")
        if "Xl" in debug:
            b.barrier(); tap_t(b, tb["X"], dbg_out["Xl"], "dbgXl")

    if PHASES >= 4:
        b.barrier()
        ph_ln2(b, tb, un)
        b.barrier()
        ph_router(b, tb, un)
        b.barrier()
        ph_ffn(b, tb, un)
        if "Xf" in debug:
            b.barrier(); tap_t(b, tb["X"], dbg_out["Xf"], "dbgXf")

    emit_output(b, tb, dbg_out)


def tap_t(b, tens, out_ap, sem):
    v = b.op("vector", lambda e: e.drain(), inc="v")
    b.wait("sync", "v", v)
    if tens.ap().dtype == F32:
        b.dma("sync", out_ap, tens[:, :], sem=sem)
    else:
        b.dma("gpsimd", out_ap, tens[:, :], sem=sem)


def rows_to_tok(b, tb, src_rows, nrows, tok_base, ps):
    TOKS, IDY = tb["TOKS"], tb["IDY"]
    b.wait("tensor", "v", b.cnt.get("v", 0))
    pe = None
    for tt in range(16):
        pe = b.op("tensor", lambda e, tt=tt: e.transpose(
            ps[0:128, tt * 16:tt * 16 + nrows],
            src_rows[0:nrows, tt * 128:(tt + 1) * 128],
            IDY[0:nrows, 0:nrows]), inc="pe" if tt == 15 else None)
    b.wait("vector", "pe", pe)
    sap = AP(ps, 0, [[T, 128], [16, 16], [1, nrows]])
    dap = AP(tb["TOKS"], tok_base, [[256, 128], [16, 16], [1, nrows]])
    r = b.op("vector", lambda e, s=sap, d=dap: e.tensor_copy(d, s), inc="v")
    b.op("vector", lambda e: e.drain())
    return r


def tok_to_rows(b, tb, tok_base, nrows, dst_rows, ps):
    TOKS, IDY = tb["TOKS"], tb["IDY"]
    b.wait("tensor", "v", b.cnt.get("v", 0))
    b.wait("tensor", "sc", b.cnt.get("sc", 0))
    pe = None
    for tt in range(16):
        pe = b.op("tensor", lambda e, tt=tt: e.transpose(
            ps[0:nrows, tt * 128:(tt + 1) * 128],
            TOKS[:, tt * 16 + tok_base: tt * 16 + tok_base + nrows],
            IDY[:, :]), inc="pe" if tt == 15 else None)
    b.wait("vector", "pe", pe)
    r = b.op("vector", lambda e, d=dst_rows, n=nrows, s=ps: e.tensor_copy(
        d, s[0:n, :]), inc="v")
    b.op("vector", lambda e: e.drain())
    return r


def tcol(tb, col, width=1):
    return AP(tb["TOKS"], col, [[256, 128], [16, 16], [1, width]])


def ph_ln_stats(b, tb, un, src_tensor, src_base_fn, src_is_bf16, xb_scr):
    SMU, SSG = tb["SMU"], tb["SSG"]
    PA = tb["PA"]
    for dc in range(DC):
        cb = src_base_fn(dc)
        if src_is_bf16:
            v = b.op("vector", lambda e, cb=cb: e.tensor_mul(
                xb_scr[:, T:2 * T], src_tensor[:, cb:cb + T],
                src_tensor[:, cb:cb + T]), inc="v")
            b.wait("tensor", "v", v)
            xbt, xbb = src_tensor, cb
        else:
            s = b.op("scalar", lambda e, cb=cb: e.activation(
                xb_scr[:, 0:T], src_tensor[:, cb:cb + T], AF.Copy), inc="sc")
            b.wait("vector", "sc", s)
            v = b.op("vector", lambda e: e.tensor_mul(
                xb_scr[:, T:2 * T], xb_scr[:, 0:T], xb_scr[:, 0:T]), inc="v")
            b.wait("tensor", "sc", s)
            b.wait("tensor", "v", v)
            xbt, xbb = xb_scr, 0
        pe = None
        for j in range(4):
            b.op("tensor", lambda e, xbt=xbt, xbb=xbb, j=j, dc=dc: e.matmul(
                PA[0:2, j * 512:(j + 1) * 512], SMU[:, :],
                xbt[:, xbb + j * 512:xbb + (j + 1) * 512],
                start=(dc == 0), stop=False), inc=None)
            pe = b.op("tensor", lambda e, j=j, dc=dc: e.matmul(
                PA[0:2, j * 512:(j + 1) * 512], SSG[:, :],
                xb_scr[:, T + j * 512:T + (j + 1) * 512],
                start=False, stop=(dc == DC - 1)), inc="pe")
        b.wait("scalar", "pe", pe)
        b.wait("vector", "pe", pe)
    return b.cnt["pe"]


C_MU, C_MS, C_T1, C_VAR, C_INV, C_MUINV, C_R1 = 0, 1, 2, 3, 4, 5, 6
C_S, C_MUS = 8, 9

RB_R1, RB_IR1, RB_R2 = 0, 1, 2


def ph_ln1(b, tb, un):
    X, U, PA, PB = tb["X"], tb["U"], tb["PA"], tb["PB"]
    STATB, SCR, LROW = tb["STATB"], tb["SCR"], tb["LROW"]
    xb_scr = un("MERG")
    stats_pe = ph_ln_stats(b, tb, un, X, lambda dc: dc * T, False, xb_scr)
    b.wait("vector", "pe", stats_pe)
    v = b.op("vector", lambda e: e.tensor_copy(LROW[0:2, :], PA[0:2, :]), inc="v")
    b.op("vector", lambda e: e.drain())
    b.wait("tensor", "v", v)
    rows_to_tok(b, tb, LROW, 2, C_MU, PB)
    TOKS = tb["TOKS"]
    def vop(fn):
        b.op("vector", fn); b.op("vector", lambda e: e.drain())
    def sop(fn):
        b.op("scalar", fn); b.op("scalar", lambda e: e.drain())
    vop(lambda e: e.tensor_mul(tcol(tb, C_T1), tcol(tb, C_MU), tcol(tb, C_MU)))
    v = b.op("vector", lambda e: e.tensor_sub(
        tcol(tb, C_VAR), tcol(tb, C_MS), tcol(tb, C_T1)), inc="v")
    b.op("vector", lambda e: e.drain())
    b.wait("scalar", "v", v)
    sop(lambda e: e.activation(tcol(tb, C_T1), tcol(tb, C_VAR), AF.Ln,
                               bias=EPS_LN))
    s = b.op("scalar", lambda e: e.activation(
        tcol(tb, C_INV), tcol(tb, C_T1), AF.Exp, scale=-0.5), inc="sc")
    b.op("scalar", lambda e: e.drain())
    b.wait("vector", "sc", s)
    vop(lambda e: e.tensor_mul(tcol(tb, C_MUINV), tcol(tb, C_MU),
                               tcol(tb, C_INV)))
    vop(lambda e: e.tensor_mul(tcol(tb, C_T1), tcol(tb, C_VAR), tcol(tb, C_INV)))
    v = b.op("vector", lambda e: e.tensor_mul(
        tcol(tb, C_T1), tcol(tb, C_T1), tcol(tb, C_INV)), inc="v")
    b.op("vector", lambda e: e.drain())
    b.wait("scalar", "v", v)
    sop(lambda e: e.activation(tcol(tb, C_T1), tcol(tb, C_T1), AF.Ln,
                               bias=EPS_RMS))
    s = b.op("scalar", lambda e: e.activation(
        tcol(tb, C_R1), tcol(tb, C_T1), AF.Exp, scale=-0.5), inc="sc")
    b.op("scalar", lambda e: e.drain())
    b.wait("vector", "sc", s)
    vop(lambda e: e.tensor_scalar_add(tcol(tb, C_T1), tcol(tb, C_INV), 1.0))
    vop(lambda e: e.reciprocal(tcol(tb, C_T1), tcol(tb, C_T1)))
    vop(lambda e: e.tensor_mul(tcol(tb, C_S), tcol(tb, C_INV), tcol(tb, C_T1)))
    v = b.op("vector", lambda e: e.tensor_mul(
        tcol(tb, C_MUS), tcol(tb, C_MU), tcol(tb, C_S)), inc="v")
    b.op("vector", lambda e: e.drain())
    b.wait("tensor", "v", v)
    tok_to_rows(b, tb, C_INV, 2, LROW[0:2, :], PB)
    v = tok_to_rows(b, tb, C_R1, 1, STATB[RB_R1:RB_R1 + 1, :], PB)
    b.wait("tensor", "v", v)
    pe = bcast16(b, tb, PA, RB_R1, STATB[0:1, :], krows=1)
    b.wait("scalar", "pe", pe)
    s = b.op("scalar", lambda e: e.activation(
        un("BCB")[:, 0:T], PA[:, :], AF.Copy), inc="sc")
    b.wait("tensor", "sc", s)
    pe = bcast2(b, tb, PA, 0, LROW)
    pe = bcast2(b, tb, PB, 1, LROW)
    b.wait("vector", "pe", pe)
    b.wait("vector", "sc", s)
    vlast = None
    for dc in range(DC):
        xc = X[:, dc * T:(dc + 1) * T]
        b.op("vector", lambda e, xc=xc: e.tensor_mul(SCR[:, 0:T], xc, PA[:, :]))
        b.op("vector", lambda e: e.drain())
        b.op("vector", lambda e: e.tensor_sub(SCR[:, 0:T], SCR[:, 0:T], PB[:, :]))
        b.op("vector", lambda e: e.drain())
        b.op("vector", lambda e, xc=xc: e.tensor_add(xc, xc, SCR[:, 0:T]))
        b.op("vector", lambda e, dc=dc: e.tensor_mul(
            U[:, dc * T:(dc + 1) * T], SCR[:, 0:T], un("BCB")[:, 0:T]))
        b.op("vector", lambda e: e.drain())
        vlast = b.op("vector", lambda e, dc=dc: e.tensor_scalar_mul(
            U[:, dc * T:(dc + 1) * T], U[:, dc * T:(dc + 1) * T],
            tb["N1C"][:, dc:dc + 1]), inc="v")
        b.op("vector", lambda e: e.drain())
    b.wait("tensor", "v", vlast)


def bcast2(b, tb, ps, which, rows2):
    pe = None
    sel = tb["SEL2"][:, which * 128:(which + 1) * 128]
    for j in range(4):
        pe = b.op("tensor", lambda e, j=j, sel=sel, r=rows2: e.matmul(
            ps[:, j * 512:(j + 1) * 512], sel, r[:, j * 512:(j + 1) * 512],
            start=True, stop=True), inc="pe")
    return pe


def bcast16(b, tb, ps, which, rows16, krows=16):
    pe = None
    sel = tb["SEL16"][0:krows, which * 128:(which + 1) * 128]
    for j in range(4):
        pe = b.op("tensor", lambda e, j=j, sel=sel, r=rows16: e.matmul(
            ps[:, j * 512:(j + 1) * 512], sel, r[:, j * 512:(j + 1) * 512],
            start=True, stop=True), inc="pe")
    return pe




UB = dict(KN=0, VT=4096, WW=8256, QN=19520, MERG=21568, PT=29760, BCB=30784)


def _wload(b, tb, dst_base, hbm_name, src_dims, dst_dims, offset=0, sem=None):
    UN = tb["UN"]
    dst = AP(UN, dst_base, [[32832, 128]] + dst_dims)
    srcs = AP(b.hbm[hbm_name].tensor, offset, src_dims)
    return b.dma("gpsimd", dst, srcs, sem=sem or ("w_" + hbm_name))


def ph_attn(b, tb, un, which):
    X, U, PA, PB = tb["X"], tb["U"], tb["PA"], tb["PB"]
    SCR, STATB, LROW, CS, UN = (tb["SCR"], tb["STATB"], tb["LROW"], tb["CS"],
                                tb["UN"])
    WWB = UB["WW"]
    wq_name, wo_name = which + "_wq", which + "_wo"

    def uns(tag, off, w):
        lo = UB[tag] + off
        return UN[:, lo:lo + w]

    b.wait("sync", "v", b.cnt.get("v", 0))
    b.dma("sync", CS[:, 0:T], b.hbm["cos_" + which], sem="cs_" + which)
    cs_ld = b.dma("sync", CS[:, T:2 * T], b.hbm["sin_" + which],
                  sem="cs_" + which)
    b.wait("gpsimd", "pe", b.cnt.get("pe", 0))
    if which == "g":
        ldk = _wload(b, tb, WWB + 3072, "g_kv",
                     [[LAT, 128], [128 * LAT, DC], [1, LAT]],
                     [[LAT, DC], [1, LAT]])
        ldku = _wload(b, tb, WWB + 5120, "g_ku",
                      [[LAT, 128], [128 * LAT, 2], [1, LAT]],
                      [[LAT, 2], [1, LAT]])
        ldvu = _wload(b, tb, WWB + 5632, "g_vu",
                      [[LAT, 128], [128 * LAT, 2], [1, LAT]],
                      [[LAT, 2], [1, LAT]])
        kq_base, kq_k, v_base = WWB + 5120, 2, WWB + 5632
    else:
        ldk = _wload(b, tb, WWB + 3072, "l_wk",
                     [[LAT, 128], [128 * LAT, DC], [1, LAT]],
                     [[LAT, DC], [1, LAT]])
        ldku = ldk
        ldvu = _wload(b, tb, WWB + 5120, "l_wv",
                      [[LAT, 128], [128 * LAT, DC], [1, LAT]],
                      [[LAT, DC], [1, LAT]])
        kq_base, kq_k, v_base = WWB + 3072, DC, WWB + 5120
    vi = b.dma("sync", uns("VT", 0, 4160), b.hbm["vinit"], sem="vi_" + which)

    if which == "g":
        b.wait("tensor", "w_g_kv", ldk)
        b.wait("tensor", "v", b.cnt.get("v", 0))
        for c in range(2):
            pe = None
            for dc in range(DC):
                for j in range(4):
                    pe = b.op("tensor", lambda e, c=c, dc=dc, j=j: e.matmul(
                        PA[:, j * 512:(j + 1) * 512],
                        UN[:, WWB + 3072 + dc * 256 + c * 128:
                           WWB + 3072 + dc * 256 + c * 128 + 128],
                        U[:, dc * T + j * 512:dc * T + (j + 1) * 512],
                        start=(dc == 0), stop=(dc == DC - 1)),
                        inc="pe" if (dc == DC - 1 and j == 3) else None)
            b.wait("scalar", "pe", pe)
            s = b.op("scalar", lambda e, c=c: e.activation(
                uns("MERG", c * T, T), PA[:, :], AF.Copy), inc="sc")
            b.wait("tensor", "sc", s)
        kv_src_t, kv_src_base = UN, UB["MERG"]
        b.wait("tensor", "w_g_ku", ldku)
        b.wait("tensor", "w_g_vu", ldvu)
    else:
        kv_src_t, kv_src_base = U, 0
        b.wait("tensor", "w_l_wk", ldk)
        b.wait("tensor", "w_l_wv", ldvu)

    b.wait("vector", "cs_" + which, cs_ld)
    for c in range(2):
        pe = None
        for lc in range(kq_k):
            for j in range(4):
                pe = b.op("tensor", lambda e, c=c, lc=lc, j=j: e.matmul(
                    PA[:, j * 512:(j + 1) * 512],
                    UN[:, kq_base + lc * 256 + c * 128:
                       kq_base + lc * 256 + c * 128 + 128],
                    kv_src_t[:, kv_src_base + lc * T + j * 512:
                             kv_src_base + lc * T + (j + 1) * 512],
                    start=(lc == 0), stop=(lc == kq_k - 1)),
                    inc="pe" if (lc == kq_k - 1 and j == 3) else None)
        b.wait("scalar", "pe", pe)
        s = b.op("scalar", lambda e: e.activation(
            uns("MERG", 2 * T, T), PA[:, :], AF.Copy), inc="sc")
        b.wait("vector", "sc", s)
        v = b.op("vector", lambda e: e.tensor_mul(
            uns("BCB", 0, T), uns("MERG", 2 * T, T), uns("MERG", 2 * T, T)),
            inc="v")
        b.op("vector", lambda e: e.drain())
        b.wait("tensor", "v", v)
        pe2 = None
        for j in range(4):
            pe2 = b.op("tensor", lambda e, j=j: e.matmul(
                PB[0:2, j * 512:(j + 1) * 512], tb["HQ2"][:, :],
                uns("BCB", j * 512, 512), start=True, stop=True),
                inc="pe" if j == 3 else None)
        b.wait("scalar", "pe", pe2)
        b.op("scalar", lambda e: e.activation(PB[0:2, :], PB[0:2, :], AF.Ln,
                                              bias=EPS_RMS))
        b.op("scalar", lambda e: e.drain())
        s = b.op("scalar", lambda e: e.activation(
            STATB[0:2, :], PB[0:2, :], AF.Exp, scale=-0.5), inc="sc")
        b.wait("tensor", "sc", s)
        pe3 = None
        for j in range(4):
            pe3 = b.op("tensor", lambda e, j=j: e.matmul(
                PB[:, j * 512:(j + 1) * 512], tb["SELP"][0:2, 0:128],
                STATB[0:2, j * 512:(j + 1) * 512], start=True, stop=True),
                inc="pe" if j == 3 else None)
        b.wait("vector", "sc", b.cnt["sc"])
        b.op("vector", lambda e: e.stream_shuffle(
            uns("BCB", 0, T), uns("MERG", 2 * T, T),
            [i ^ 1 for i in range(32)]))
        b.op("vector", lambda e: e.drain())
        b.op("vector", lambda e: e.tensor_mul(
            SCR[:, 0:T], uns("MERG", 2 * T, T), CS[:, 0:T]))
        b.op("vector", lambda e: e.drain())
        b.op("vector", lambda e: e.tensor_mul(
            uns("BCB", 0, T), uns("BCB", 0, T), CS[:, T:2 * T]))
        b.op("vector", lambda e: e.drain())
        b.op("vector", lambda e: e.tensor_add(
            SCR[:, 0:T], SCR[:, 0:T], uns("BCB", 0, T)))
        b.op("vector", lambda e: e.drain())
        b.wait("vector", "pe", pe3)
        v = b.op("vector", lambda e, c=c: e.tensor_mul(
            uns("KN", c * T, T), SCR[:, 0:T], PB[:, :]), inc="v")
        b.op("vector", lambda e: e.drain())
        b.wait("tensor", "v", v)

    b.wait("vector", "vi_" + which, vi)
    for c in range(2):
        pe = None
        for lc in range(kq_k):
            for j in range(4):
                pe = b.op("tensor", lambda e, c=c, lc=lc, j=j: e.matmul(
                    PA[:, j * 512:(j + 1) * 512],
                    UN[:, v_base + lc * 256 + c * 128:
                       v_base + lc * 256 + c * 128 + 128],
                    kv_src_t[:, kv_src_base + lc * T + j * 512:
                             kv_src_base + lc * T + (j + 1) * 512],
                    start=(lc == 0), stop=(lc == kq_k - 1)),
                    inc="pe" if (lc == kq_k - 1 and j == 3) else None)
        b.wait("scalar", "pe", pe)
        s = b.op("scalar", lambda e: e.activation(
            tb["SCR"][:, 0:T], PA[:, :], AF.Copy), inc="sc")
        b.wait("tensor", "sc", s)
        for g4 in range(4):
            pe = None
            for m in range(4):
                kt = g4 * 4 + m
                pe = b.op("tensor", lambda e, kt=kt, m=m: e.transpose(
                    PB[:, m * 512:m * 512 + 128],
                    tb["SCR"][:, kt * 128:kt * 128 + 128], tb["IDY"][:, :]),
                    inc="pe" if m == 3 else None)
            b.wait("vector", "pe", pe)
            vv = None
            for m in range(4):
                kt = g4 * 4 + m
                for half in range(2):
                    h = 2 * c + half
                    dst = UN[:, UB["VT"] + (kt * KVH + h) * 65:
                             UB["VT"] + (kt * KVH + h) * 65 + 64]
                    vv = b.op("vector", lambda e, m=m, half=half, dst=dst:
                              e.tensor_copy(dst, PB[:, m * 512 + half * 64:
                                                    m * 512 + half * 64 + 64]),
                              inc="v" if (m == 3 and half == 1) else None)
            b.wait("tensor", "v", vv)

    b.wait("gpsimd", "pe", b.cnt["pe"])
    UNt = tb["UN"]
    for (pbase, roff) in ((0, 0), (0, 512), (64, 256), (64, 768)):
        cp0 = 0 if roff in (0, 256) else 4
        dst = AP(UNt, pbase * 32832 + WWB + 3072 + cp0 * 1024,
                 [[32832, 64], [1024, 4], [1, D]])
        srcs = AP(b.hbm[wo_name].tensor, roff * D,
                  [[D, 64], [64 * D, 4], [1, D]])
        ldwo = b.dma("gpsimd", dst, srcs, sem="w_" + wo_name)

    ring_guard = {}
    blk_id = [0]
    for half in range(2):
        hq0 = half * 1024
        ld_blk = {}
        def load_wq_blk(j0):
            hp = PAIRS[j0]
            r = None
            for hf in range(2):
                r = _wload(
                    b, tb, WWB + (j0 % 3) * 1024 + hf * 64, wq_name,
                    [[D, 128], [128 * D, DC], [1, 64]],
                    [[128, DC], [1, 64]], offset=hp * 64 + hf * 256,
                    sem="wq%s%d" % (which, j0 % 3))
            return r
        b.wait("gpsimd", "pe", b.cnt.get("pe", 0))
        for j0 in range(3):
            ld_blk[j0] = load_wq_blk(j0)
        for j in range(DC):
            b.wait("tensor", "wq%s%d" % (which, j % 3), ld_blk[j])
            wqb = WWB + (j % 3) * 1024
            pe = None
            for dc in range(DC):
                for jj in range(2):
                    pe = b.op("tensor", lambda e, dc=dc, jj=jj, wqb=wqb,
                              hq0=hq0: e.matmul(
                                  PB[:, jj * 512:(jj + 1) * 512],
                                  UN[:, wqb + dc * 128:wqb + dc * 128 + 128],
                                  U[:, dc * T + hq0 + jj * 512:
                                     dc * T + hq0 + (jj + 1) * 512],
                                  start=(dc == 0), stop=(dc == DC - 1)),
                              inc="pe" if (dc == DC - 1 and jj == 1) else None)
            if j + 3 < DC:
                b.wait("gpsimd", "pe", pe)
                ld_blk[j + 3] = load_wq_blk(j + 3)
            b.wait("scalar", "pe", pe)
            s = b.op("scalar", lambda e: e.activation(
                uns("QN", 1024, 1024), PB[:, 0:1024], AF.Copy), inc="sc")
            b.wait("vector", "sc", s)
            v = b.op("vector", lambda e: e.tensor_mul(
                uns("BCB", 0, 1024), uns("QN", 1024, 1024),
                uns("QN", 1024, 1024)), inc="v")
            b.op("vector", lambda e: e.drain())
            b.wait("tensor", "v", v)
            pe2 = None
            for jj in range(2):
                pe2 = b.op("tensor", lambda e, jj=jj: e.matmul(
                    PA[0:2, jj * 512:(jj + 1) * 512], tb["HQ2"][:, :],
                    uns("BCB", jj * 512, 512), start=True, stop=True),
                    inc="pe" if jj == 1 else None)
            b.wait("scalar", "pe", pe2)
            b.op("scalar", lambda e: e.activation(
                PA[0:2, 0:1024], PA[0:2, 0:1024], AF.Ln, bias=EPS_RMS))
            b.op("scalar", lambda e: e.drain())
            s = b.op("scalar", lambda e: e.activation(
                STATB[0:2, 0:1024], PA[0:2, 0:1024], AF.Exp, scale=-0.5),
                inc="sc")
            b.wait("tensor", "sc", s)
            pe3 = None
            for jj in range(2):
                pe3 = b.op("tensor", lambda e, jj=jj: e.matmul(
                    PA[:, jj * 512:(jj + 1) * 512], tb["SELP"][0:2, 0:128],
                    STATB[0:2, jj * 512:(jj + 1) * 512], start=True,
                    stop=True), inc="pe" if jj == 1 else None)
            b.wait("vector", "sc", b.cnt["sc"])
            b.op("vector", lambda e: e.stream_shuffle(
                uns("BCB", 0, 1024), uns("QN", 1024, 1024),
                [i ^ 1 for i in range(32)]))
            b.op("vector", lambda e: e.drain())
            b.op("vector", lambda e, hq0=hq0: e.tensor_mul(
                SCR[:, 0:1024], uns("QN", 1024, 1024), CS[:, hq0:hq0 + 1024]))
            b.op("vector", lambda e: e.drain())
            b.op("vector", lambda e, hq0=hq0: e.tensor_mul(
                uns("BCB", 0, 1024), uns("BCB", 0, 1024),
                CS[:, T + hq0:T + hq0 + 1024]))
            b.op("vector", lambda e: e.drain())
            b.op("vector", lambda e: e.tensor_add(
                SCR[:, 0:1024], SCR[:, 0:1024], uns("BCB", 0, 1024)))
            b.op("vector", lambda e: e.drain())
            b.wait("vector", "pe", pe3)
            v = b.op("vector", lambda e: e.tensor_mul(
                uns("QN", 0, 1024), SCR[:, 0:1024], PA[:, 0:1024]), inc="v")
            b.op("vector", lambda e: e.drain())
            b.wait("tensor", "v", v)
            for hh in range(2):
                kvh = (j // 4) * 2 + hh
                ktile, krow = j // 4, hh * 64
                ob = 2 + hh
                for qc2 in range(2):
                    qcg = half * 2 + qc2
                    plan = sdpa_plan(which, qcg)
                    nplan = len(plan)
                    for ip, (kt, c0, cw, masks, zfrom) in enumerate(plan):
                        sb = qc2 % 2
                        pe = b.op("tensor", lambda e, kt=kt, c0=c0, cw=cw,
                                  hh=hh, qc2=qc2, ktile=ktile, krow=krow,
                                  sb=sb: e.matmul(
                                      PB[:, sb * 512 + c0:sb * 512 + c0 + cw],
                                      UN[krow:krow + 64,
                                         UB["KN"] + ktile * T + kt * 128:
                                         UB["KN"] + ktile * T + kt * 128 + 128],
                                      UN[hh * 64:hh * 64 + 64,
                                         UB["QN"] + qc2 * 512 + c0:
                                         UB["QN"] + qc2 * 512 + c0 + cw],
                                      start=True, stop=True), inc="pe")
                        b.wait("scalar", "pe", pe)
                        s = b.op("scalar", lambda e, sb=sb, c0=c0, cw=cw,
                                 kt=kt: e.activation(
                                     UN[:, UB["PT"] + (kt % 2) * 512:
                                        UB["PT"] + (kt % 2) * 512 + cw],
                                     PB[:, sb * 512 + c0:sb * 512 + c0 + cw],
                                     AF.Exp, scale=0.125), inc="sc")
                        if masks or zfrom is not None:
                            b.wait("vector", "sc", s)
                            vm = None
                            if zfrom is not None:
                                vm = b.op("vector", lambda e, kt=kt,
                                          zfrom=zfrom: e.tensor_scalar_mul(
                                              UN[:, UB["PT"] + (kt % 2) * 512
                                                 + zfrom:
                                                 UB["PT"] + (kt % 2) * 512
                                                 + 512],
                                              UN[:, UB["PT"] + (kt % 2) * 512
                                                 + zfrom:
                                                 UB["PT"] + (kt % 2) * 512
                                                 + 512], 0.0), inc="v")
                                b.op("vector", lambda e: e.drain())
                            for (mt, off) in masks:
                                w = min(128, cw - off)
                                vm = b.op("vector", lambda e, kt=kt, mt=mt,
                                          off=off, w=w: e.tensor_mul(
                                              UN[:, UB["PT"] + (kt % 2) * 512
                                                 + off:
                                                 UB["PT"] + (kt % 2) * 512
                                                 + off + w],
                                              UN[:, UB["PT"] + (kt % 2) * 512
                                                 + off:
                                                 UB["PT"] + (kt % 2) * 512
                                                 + off + w],
                                              tb["TRI"][:, mt * 128:
                                                        mt * 128 + w]),
                                          inc="v")
                                b.op("vector", lambda e: e.drain())
                            b.wait("tensor", "v", vm)
                        else:
                            b.wait("tensor", "sc", s)
                        pe = b.op("tensor", lambda e, kt=kt, c0=c0, cw=cw,
                                  kvh=kvh, ob=ob, ip=ip, nplan=nplan:
                                  e.matmul(
                                      PB[0:65, ob * 512 + c0:
                                         ob * 512 + c0 + cw],
                                      UN[:, UB["VT"] + (kt * KVH + kvh) * 65:
                                         UB["VT"] + (kt * KVH + kvh) * 65
                                         + 65],
                                      UN[:, UB["PT"] + (kt % 2) * 512:
                                         UB["PT"] + (kt % 2) * 512 + cw],
                                      start=(ip == 0), stop=(ip == nplan - 1)),
                                  inc="pe")
                    b.wait("vector", "pe", b.cnt["pe"])
                    v = b.op("vector", lambda e, ob=ob, qc2=qc2: e.reciprocal(
                        UN[64:65, UB["BCB"] + qc2 * 512:
                           UB["BCB"] + qc2 * 512 + 512],
                        PB[64:65, ob * 512:ob * 512 + 512]), inc="v")
                    b.op("vector", lambda e: e.drain())
                    v = b.op("vector", lambda e, ob=ob, hh=hh, j=j, qc2=qc2:
                             e.tensor_copy(
                                 UN[hh * 64:hh * 64 + 64,
                                    UB["MERG"] + j * 1024 + qc2 * 512:
                                    UB["MERG"] + j * 1024 + qc2 * 512 + 512],
                                 PB[0:64, ob * 512:ob * 512 + 512]), inc="v")
                    b.op("vector", lambda e: e.drain())
                    b.wait("tensor", "v", v)
                pe = None
                for jj in range(2):
                    pe = b.op("tensor", lambda e, jj=jj: e.matmul(
                        PA[:, jj * 512:(jj + 1) * 512],
                        tb["ONES128"][64:65, :],
                        UN[64:65, UB["BCB"] + jj * 512:
                           UB["BCB"] + jj * 512 + 512],
                        start=True, stop=True), inc="pe")
                b.wait("vector", "pe", pe)
                v = b.op("vector", lambda e, hh=hh, j=j: e.tensor_mul(
                    UN[hh * 64:hh * 64 + 64,
                       UB["MERG"] + j * 1024:UB["MERG"] + j * 1024 + 1024],
                    UN[hh * 64:hh * 64 + 64,
                       UB["MERG"] + j * 1024:UB["MERG"] + j * 1024 + 1024],
                    PA[hh * 64:hh * 64 + 64, 0:1024]), inc="v")
                b.op("vector", lambda e: e.drain())
                b.wait("tensor", "v", v)
        b.wait("tensor", "w_" + wo_name, ldwo)
        if which == "g" and half == 0:
            v = tok_to_rows(b, tb, C_S, 2, LROW[0:2, :], PA)
            b.wait("tensor", "v", v)
        sb_tiles = [False]
        for doutc in range(DC):
            psl = [0, 2][doutc % 2]
            pe = None
            for jp in range(DC):
                for jj in range(2):
                    pe = b.op("tensor", lambda e, doutc=doutc, jp=jp, jj=jj,
                              psl=psl: e.matmul(
                                  PA[:, (psl + jj) * 512:
                                     (psl + jj + 1) * 512],
                                  UN[:, WWB + 3072 + jp * 1024 + doutc * 128:
                                     WWB + 3072 + jp * 1024 + doutc * 128
                                     + 128],
                                  UN[:, UB["MERG"] + jp * 1024 + jj * 512:
                                     UB["MERG"] + jp * 1024 + (jj + 1) * 512],
                                  start=(jp == 0), stop=(jp == DC - 1)),
                              inc="pe" if (jp == DC - 1 and jj == 1) else None)
            b.wait("vector", "pe", pe)
            if which == "g" and not sb_tiles[0]:
                sb_tiles[0] = True
                pe2 = None
                for jj in range(2):
                    b.op("tensor", lambda e, jj=jj, hq0=hq0: e.matmul(
                        PB[:, jj * 512:(jj + 1) * 512], tb["SEL2"][:, 0:128],
                        LROW[:, hq0 + jj * 512:hq0 + (jj + 1) * 512],
                        start=True, stop=True))
                    pe2 = b.op("tensor", lambda e, jj=jj, hq0=hq0: e.matmul(
                        PB[:, (2 + jj) * 512:(3 + jj) * 512],
                        tb["SEL2"][:, 128:256],
                        LROW[:, hq0 + jj * 512:hq0 + (jj + 1) * 512],
                        start=True, stop=True), inc="pe")
                b.wait("vector", "pe", pe2)
            xsl = X[:, doutc * T + hq0: doutc * T + hq0 + 1024]
            if which == "g":
                b.op("vector", lambda e, xsl=xsl: e.tensor_mul(
                    SCR[:, 0:1024], xsl, PB[:, 0:1024]))
                b.op("vector", lambda e: e.drain())
                b.op("vector", lambda e: e.tensor_sub(
                    SCR[:, 0:1024], SCR[:, 0:1024], PB[:, 1024:2048]))
                b.op("vector", lambda e: e.drain())
                b.op("vector", lambda e, doutc=doutc, hq0=hq0, psl=psl:
                     e.tensor_add(
                         U[:, doutc * T + hq0:doutc * T + hq0 + 1024],
                         SCR[:, 0:1024], PA[:, psl * 512:psl * 512 + 1024]))
                b.op("vector", lambda e: e.drain())
            v = b.op("vector", lambda e, xsl=xsl, psl=psl: e.tensor_add(
                xsl, xsl, PA[:, psl * 512:psl * 512 + 1024]), inc="v")
            b.op("vector", lambda e: e.drain())
            b.wait("tensor", "v", v)
        sb_tiles[0] = False


def sdpa_plan(which, qcg):
    qg0 = qcg * 512
    plan = []
    if which == "g":
        for kt in range(4 * qcg + 4):
            lo = kt * 128 - qg0
            c0 = max(0, lo)
            masks = [(0, 0)] if lo >= 0 else []
            plan.append((kt, c0, 512 - c0, masks, None))
    else:
        for kt in range(max(0, 4 * qcg - 2), 4 * qcg + 4):
            lo = kt * 128 - qg0
            c0 = max(0, lo)
            hi = min(512, lo + 383 + 1)
            if c0 >= 512 or hi <= c0:
                continue
            masks = []
            if lo >= 0:
                masks.append((0, 0))
            if 0 <= lo + 256 < hi:
                masks.append((1, lo + 256 - c0))
            plan.append((kt, c0, hi - c0, masks, None))
        kt, c0, cw, masks, _ = plan[0]
        assert c0 == 0
        if cw < 512:
            plan[0] = (kt, 0, 512, masks, cw)
    return plan


def ph_mid(b, tb, un):
    U, PA, PB, STATB = tb["U"], tb["PA"], tb["PB"], tb["STATB"]
    UN = tb["UN"]
    for dc in range(DC):
        v = b.op("vector", lambda e, dc=dc: e.tensor_mul(
            UN[:, UB["BCB"]:UB["BCB"] + T], U[:, dc * T:(dc + 1) * T],
            U[:, dc * T:(dc + 1) * T]), inc="v")
        b.op("vector", lambda e: e.drain())
        b.wait("tensor", "v", v)
        pe = None
        for j in range(4):
            pe = b.op("tensor", lambda e, j=j, dc=dc: e.matmul(
                PA[0:2, j * 512:(j + 1) * 512], tb["SMU"][:, :],
                UN[:, UB["BCB"] + j * 512:UB["BCB"] + (j + 1) * 512],
                start=(dc == 0), stop=(dc == DC - 1)), inc="pe")
        b.wait("vector", "pe", pe)
    b.wait("scalar", "pe", b.cnt["pe"])
    b.op("scalar", lambda e: e.activation(PA[0:1, :], PA[0:1, :], AF.Ln,
                                          bias=EPS_RMS))
    b.op("scalar", lambda e: e.drain())
    s = b.op("scalar", lambda e: e.activation(
        STATB[0:1, :], PA[0:1, :], AF.Exp, scale=-0.5), inc="sc")
    b.wait("tensor", "sc", s)
    pe = bcast16(b, tb, PA, 0, STATB[0:1, :], krows=1)
    b.wait("vector", "pe", pe)
    vlast = None
    for dc in range(DC):
        b.op("vector", lambda e, dc=dc: e.tensor_mul(
            U[:, dc * T:(dc + 1) * T], U[:, dc * T:(dc + 1) * T], PA[:, :]))
        b.op("vector", lambda e: e.drain())
        vlast = b.op("vector", lambda e, dc=dc: e.tensor_scalar_mul(
            U[:, dc * T:(dc + 1) * T], U[:, dc * T:(dc + 1) * T],
            tb["N2C"][:, dc:dc + 1]), inc="v")
        b.op("vector", lambda e: e.drain())
    b.wait("tensor", "v", vlast)




FB = dict(W1R=0, W2R=8192, CEB=12288, H=14336, RW=30720, B2=30848)


def ph_ln2(b, tb, un):
    X, U, PA, PB, LROW, SCR = (tb["X"], tb["U"], tb["PA"], tb["PB"],
                               tb["LROW"], tb["SCR"])
    UN = tb["UN"]
    xb_scr = UN[:, FB["H"]:FB["H"] + 2 * T]
    stats_pe = ph_ln_stats(b, tb, un, X, lambda dc: dc * T, False, xb_scr)
    b.wait("vector", "pe", stats_pe)
    v = b.op("vector", lambda e: e.tensor_copy(LROW[0:2, :], PA[0:2, :]),
             inc="v")
    b.op("vector", lambda e: e.drain())
    b.wait("tensor", "v", v)
    rows_to_tok(b, tb, LROW, 2, C_MU, PB)
    TOKS = tb["TOKS"]
    def vop(fn):
        b.op("vector", fn); b.op("vector", lambda e: e.drain())
    def sop(fn):
        b.op("scalar", fn); b.op("scalar", lambda e: e.drain())
    vop(lambda e: e.tensor_mul(tcol(tb, C_T1), tcol(tb, C_MU), tcol(tb, C_MU)))
    v = b.op("vector", lambda e: e.tensor_sub(
        tcol(tb, C_VAR), tcol(tb, C_MS), tcol(tb, C_T1)), inc="v")
    b.op("vector", lambda e: e.drain())
    b.wait("scalar", "v", v)
    sop(lambda e: e.activation(tcol(tb, C_T1), tcol(tb, C_VAR), AF.Ln,
                               bias=EPS_LN))
    s = b.op("scalar", lambda e: e.activation(
        tcol(tb, C_INV), tcol(tb, C_T1), AF.Exp, scale=-0.5), inc="sc")
    b.op("scalar", lambda e: e.drain())
    b.wait("vector", "sc", s)
    v = b.op("vector", lambda e: e.tensor_mul(
        tcol(tb, C_MUINV), tcol(tb, C_MU), tcol(tb, C_INV)), inc="v")
    b.op("vector", lambda e: e.drain())
    b.wait("tensor", "v", v)
    v = tok_to_rows(b, tb, C_INV, 2, LROW[0:2, :], PB)
    b.wait("tensor", "v", v)
    pe = bcast2(b, tb, PA, 0, LROW)
    pe = bcast2(b, tb, PB, 1, LROW)
    b.wait("vector", "pe", pe)
    vlast = None
    for dc in range(DC):
        xc = X[:, dc * T:(dc + 1) * T]
        b.op("vector", lambda e, xc=xc: e.tensor_mul(SCR[:, 0:T], xc, PA[:, :]))
        b.op("vector", lambda e: e.drain())
        vlast = b.op("vector", lambda e, dc=dc: e.tensor_sub(
            U[:, dc * T:(dc + 1) * T], SCR[:, 0:T], PB[:, :]), inc="v")
        b.op("vector", lambda e: e.drain())
    b.wait("tensor", "v", vlast)


def ph_router(b, tb, un):
    U, PA, PB, SCR, STATB = tb["U"], tb["PA"], tb["PB"], tb["SCR"], tb["STATB"]
    UN, TOKS = tb["UN"], tb["TOKS"]
    b.wait("gpsimd", "v", b.cnt.get("v", 0))
    ldrw = _wload(b, tb, FB["RW"], "rw",
                  [[NE, 128], [128 * NE, DC], [1, NE]], [[NE, DC], [1, NE]])
    ldb2 = b.dma("gpsimd", AP(UN, FB["B2"], [[32832, NE], [1, D]]),
                 b.hbm["b2"], sem="w_b2")
    b.wait("sync", "v", b.cnt.get("v", 0))
    ldb1 = b.dma("sync", SCR[0:NE, 0:I_FF], b.hbm["b1"], sem="w_b1")
    ldrb = b.dma("sync", tb["RBC"][:, :], b.hbm["rb"], sem="w_rb")
    b.wait("tensor", "w_b1", ldb1)
    b.wait("tensor", "v", b.cnt.get("v", 0))
    pe = None
    for c in range(NE):
        pe = b.op("tensor", lambda e, c=c: e.transpose(
            PB[:, c * 16:c * 16 + 16], SCR[0:NE, c * 128:(c + 1) * 128],
            tb["IDY"][0:NE, 0:NE]), inc="pe" if c == NE - 1 else None)
    b.wait("vector", "pe", pe)
    v = b.op("vector", lambda e: e.tensor_copy(tb["B1T"][:, :], PB[:, 0:256]),
             inc="v")
    b.op("vector", lambda e: e.drain())

    b.wait("tensor", "w_rw", ldrw)
    b.wait("tensor", "v", b.cnt["v"])
    pe = None
    for dc in range(DC):
        for j in range(4):
            pe = b.op("tensor", lambda e, dc=dc, j=j: e.matmul(
                PA[0:NE, j * 512:(j + 1) * 512],
                UN[:, FB["RW"] + dc * NE:FB["RW"] + (dc + 1) * NE],
                U[:, dc * T + j * 512:dc * T + (j + 1) * 512],
                start=(dc == 0), stop=(dc == DC - 1)),
                inc="pe" if (dc == DC - 1 and j == 3) else None)
    b.wait("vector", "pe", pe)
    b.wait("vector", "w_rb", ldrb)
    v = b.op("vector", lambda e: e.tensor_scalar(
        SCR[0:NE, 0:T], PA[0:NE, :], tb["RBC"][:, 0:1], None, ALU.add),
        inc="v")
    b.op("vector", lambda e: e.drain())
    b.wait("scalar", "v", v)
    s = b.op("scalar", lambda e: e.activation(
        SCR[0:NE, 0:T], SCR[0:NE, 0:T], AF.Exp), inc="sc")
    b.op("scalar", lambda e: e.drain())
    b.wait("tensor", "sc", s)
    v = rows_to_tok(b, tb, SCR[0:NE, :], NE, 0, PB)
    b.wait("sync", "pe", b.cnt["pe"])
    ldng = b.dma("sync", SCR[:, 768:1024], b.hbm["neg_big"], sem="w_ng")
    W = SCR[:, 0:256]
    Z = SCR[:, 256:512]
    CE = SCR[:, 512:768]
    RM = lambda a, b2: SCR[:, 1024 + a:1024 + b2]
    w3 = AP(SCR, 0, [[2048, 128], [16, 16], [1, 16]])
    m3 = AP(SCR, 1024, [[2048, 128], [1, 16], [0, 16]])
    b.wait("vector", "w_ng", ldng)
    v = b.op("vector", lambda e: e.tensor_copy(W, TOKS[:, :]), inc="v")
    b.op("vector", lambda e: e.drain())
    b.op("vector", lambda e: e.tensor_scalar_mul(Z, W, 0.0))
    b.op("vector", lambda e: e.drain())
    for rnd in range(TOPK):
        b.op("vector", lambda e: e.reduce_max(RM(0, 16), w3,
                                              axis=mybir.AxisListType.X))
        b.op("vector", lambda e: e.drain())
        if rnd == 0:
            b.op("vector", lambda e: e.tensor_copy(RM(16, 32), RM(0, 16)))
        else:
            b.op("vector", lambda e: e.tensor_add(RM(16, 32), RM(16, 32),
                                                  RM(0, 16)))
        b.op("vector", lambda e: e.drain())
        b.op("vector", lambda e: e.tensor_tensor(
            AP(tb["RMSK"], 0, [[256, 128], [16, 16], [1, 16]]), w3, m3,
            ALU.is_ge))
        b.op("vector", lambda e: e.drain())
        b.op("vector", lambda e: e.select(
            W, tb["RMSK"][:, :], SCR[:, 768:1024], W, add_drain=True))
        b.op("vector", lambda e: e.drain())
    b.op("vector", lambda e: e.tensor_scalar(
        tb["RMSK"][:, :], W, 0.0, None, ALU.is_lt))
    b.op("vector", lambda e: e.drain())
    b.op("vector", lambda e: e.select(CE, tb["RMSK"][:, :], TOKS[:, :], Z,
                                      add_drain=True))
    b.op("vector", lambda e: e.drain())
    b.op("vector", lambda e: e.reciprocal(RM(16, 32), RM(16, 32)))
    b.op("vector", lambda e: e.drain())
    ce3 = AP(SCR, 512, [[2048, 128], [16, 16], [1, 16]])
    s3 = AP(SCR, 1040, [[2048, 128], [1, 16], [0, 16]])
    v = b.op("vector", lambda e: e.tensor_tensor(ce3, ce3, s3, ALU.mult),
             inc="v")
    b.op("vector", lambda e: e.drain())
    v = b.op("vector", lambda e: e.tensor_copy(TOKS[:, :], CE), inc="v")
    b.op("vector", lambda e: e.drain())
    b.wait("tensor", "v", v)
    v = tok_to_rows(b, tb, 0, NE, STATB[0:NE, :], PB)
    b.wait("tensor", "v", v)
    b.wait("tensor", "w_b2", ldb2)


def ph_ffn(b, tb, un):
    X, U, PA, PB, STATB = tb["X"], tb["U"], tb["PA"], tb["PB"], tb["STATB"]
    UN = tb["UN"]
    NEXP = int(os.environ.get("KFFN_E", str(NE)))
    w1_guard = {}
    w2_guard = {}

    def load_w1_quarter(e, q):
        slot = (e * 4 + q) % 2
        key = e * 4 + q
        if key - 2 in w1_guard and w1_guard[key - 2]:
            b.wait("gpsimd", "pe", w1_guard[key - 2])
        r = None
        for iq in range(4):
            ic = q * 4 + iq
            r = b.dma("gpsimd",
                      AP(UN, FB["W1R"] + slot * 4096 + iq * 1024,
                         [[32832, 128], [128, DC], [1, 128]]),
                      AP(b.hbm["w1"].tensor, e * D * I_FF + ic * 128,
                         [[I_FF, 128], [128 * I_FF, DC], [1, 128]]),
                      sem="w1s%d" % slot)
        return r

    def load_w2_slab(e, ic, jjkey=0):
        slot = ic % 4
        return b.dma("gpsimd",
                     AP(UN, FB["W2R"] + slot * 1024,
                        [[32832, 128], [1, D]]),
                     AP(b.hbm["w2"].tensor, e * I_FF * D + ic * 128 * D,
                        [[D, 128], [1, D]]),
                     sem="w2s%d" % slot)

    for th in range(2):
        h0 = th * 1024
        ldq = {}
        lds = {}
        b.wait("gpsimd", "pe", b.cnt.get("pe", 0))
        ldq[0] = load_w1_quarter(0, 0)
        ldq[1] = load_w1_quarter(0, 1)
        for ic in range(4):
            lds[(0, 0, ic)] = load_w2_slab(0, ic)
        for e in range(NEXP):
            ceb = FB["CEB"] + (e % 2) * 1024
            b.wait("tensor", "v", b.cnt.get("v", 0))
            pe = None
            for jj in range(2):
                pe = b.op("tensor", lambda e2, jj=jj, e=e, h0=h0: e2.matmul(
                    PA[:, jj * 512:(jj + 1) * 512],
                    tb["SEL16"][:, e * 128:(e + 1) * 128],
                    STATB[0:NE, h0 + jj * 512:h0 + (jj + 1) * 512],
                    start=True, stop=True), inc="pe")
            b.wait("scalar", "pe", pe)
            s = b.op("scalar", lambda e2, ceb=ceb: e2.activation(
                UN[:, ceb:ceb + 1024], PA[:, 0:1024], AF.Copy), inc="sc")
            b.wait("vector", "sc", s)
            b.wait("tensor", "sc", s)
            for ic in range(NE):
                q, iq = ic // 4, ic % 4
                slot = (e * 4 + q) % 2
                if iq == 0:
                    b.wait("tensor", "w1s%d" % slot, ldq[e * 4 + q])
                psl = (ic % 2) * 2
                pe = None
                for dc in range(DC):
                    for jj in range(2):
                        pe = b.op("tensor", lambda e2, ic=ic, dc=dc, jj=jj,
                                  slot=slot, iq=iq, h0=h0, psl=psl: e2.matmul(
                                      PA[:, (psl + jj) * 512:
                                         (psl + jj + 1) * 512],
                                      UN[:, FB["W1R"] + slot * 4096 +
                                         iq * 1024 + dc * 128:
                                         FB["W1R"] + slot * 4096 +
                                         iq * 1024 + dc * 128 + 128],
                                      U[:, dc * T + h0 + jj * 512:
                                         dc * T + h0 + (jj + 1) * 512],
                                      start=(dc == 0), stop=(dc == DC - 1)),
                                  inc="pe" if (dc == DC - 1 and jj == 1)
                                  else None)
                w1_guard[e * 4 + q] = pe
                if iq == 3:
                    nk = e * 4 + q + 2
                    if nk < NEXP * 4:
                        ldq[nk] = load_w1_quarter(nk // 4, nk % 4)
                b.wait("scalar", "pe", pe)
                ss = None
                for jj in range(2):
                    ss = b.op("scalar", lambda e2, ic=ic, jj=jj, e=e, psl=psl:
                              e2.activation(
                                  UN[:, FB["H"] + ic * 1024 + jj * 512:
                                     FB["H"] + ic * 1024 + (jj + 1) * 512],
                                  PA[:, (psl + jj) * 512:(psl + jj + 1) * 512],
                                  AF.Gelu,
                                  bias=tb["B1T"][:, ic * NE + e:
                                                 ic * NE + e + 1]),
                              inc="sc")
                b.wait("vector", "sc", ss)
                v = b.op("vector", lambda e2, ic=ic, ceb=ceb: e2.tensor_mul(
                    UN[:, FB["H"] + ic * 1024:FB["H"] + (ic + 1) * 1024],
                    UN[:, FB["H"] + ic * 1024:FB["H"] + (ic + 1) * 1024],
                    UN[:, ceb:ceb + 1024]), inc="v")
                b.op("vector", lambda e2: e2.drain())
                b.wait("tensor", "v", v)
            for jj in range(2):
                for ic in range(NE):
                    slot = ic % 4
                    b.wait("tensor", "w2s%d" % slot, lds[(e, jj, ic)])
                    pe = None
                    for dout in range(DC):
                        ps_t = PA if dout < 4 else PB
                        bank = dout % 4
                        last = (ic == NE - 1 and e != NEXP - 1)
                        pe = b.op("tensor", lambda e2, ic=ic, dout=dout,
                                  jj=jj, ps_t=ps_t, bank=bank, last=last:
                                  e2.matmul(
                                      ps_t[:, bank * 512:(bank + 1) * 512],
                                      UN[:, FB["W2R"] + (ic % 4) * 1024 +
                                         dout * 128:
                                         FB["W2R"] + (ic % 4) * 1024 +
                                         dout * 128 + 128],
                                      UN[:, FB["H"] + ic * 1024 + jj * 512:
                                         FB["H"] + ic * 1024 + (jj + 1) * 512],
                                      start=(ic == 0), stop=last),
                                  inc="pe" if dout == DC - 1 else None)
                    if ic + 4 < NE:
                        nk = (e, jj, ic + 4)
                    elif jj == 0:
                        nk = (e, 1, ic - 12)
                    elif e + 1 < NEXP:
                        nk = (e + 1, 0, ic - 12)
                    else:
                        nk = None
                    if nk is not None:
                        b.wait("gpsimd", "pe", pe)
                        lds[nk] = load_w2_slab(nk[0], nk[2], jjkey=nk[1])
                if e == NEXP - 1:
                    pe = None
                    for dout in range(DC):
                        ps_t = PA if dout < 4 else PB
                        bank = dout % 4
                        pe = b.op("tensor", lambda e2, dout=dout, jj=jj,
                                  ps_t=ps_t, bank=bank, h0=h0: e2.matmul(
                                      ps_t[:, bank * 512:(bank + 1) * 512],
                                      UN[0:NE, FB["B2"] + dout * 128:
                                         FB["B2"] + dout * 128 + 128],
                                      STATB[0:NE, h0 + jj * 512:
                                            h0 + (jj + 1) * 512],
                                      start=False, stop=True), inc="pe")
                b.wait("vector", "pe", pe)
                vlast = None
                for dout in range(DC):
                    ps_t = PA if dout < 4 else PB
                    bank = dout % 4
                    vlast = b.op("vector", lambda e2, dout=dout, jj=jj,
                                 h0=h0, ps_t=ps_t, bank=bank: e2.tensor_add(
                                     X[:, dout * T + h0 + jj * 512:
                                        dout * T + h0 + (jj + 1) * 512],
                                     X[:, dout * T + h0 + jj * 512:
                                        dout * T + h0 + (jj + 1) * 512],
                                     ps_t[:, bank * 512:(bank + 1) * 512]),
                                 inc="v")
                    b.op("vector", lambda e2: e2.drain())
                b.wait("tensor", "v", vlast)


def emit_output(b, tb, dbg_out):
    X, PA, PB = tb["X"], tb["PA"], tb["PB"]
    b.barrier()
    last_store = None
    st_done = {}
    for tt in range(T // 128):
        par = tt % 2
        ps = [PA, PB][par]
        stg = tb["SCR"][:, par * D:(par + 1) * D]
        if tt >= 2:
            b.wait("vector", "st%d" % par, st_done[tt - 2])
        pe_done = None
        for dc in range(DC):
            src = X[:, dc * T + tt * 128: dc * T + (tt + 1) * 128]
            pe_done = b.op("tensor", lambda e, s=src, dc=dc, ps=ps: e.transpose(
                ps[:, dc * 128:(dc + 1) * 128], s, tb["IDY"][:, :]),
                inc="pe" if dc == DC - 1 else None)
        b.wait("vector", "pe", pe_done)
        vv = b.op("vector", lambda e, ps=ps, s=stg: e.tensor_copy(s, ps[:, 0:D]),
                  inc="v")
        b.wait("sync", "v", vv)
        last_store = b.dma("sync", b.hbm["yout"][tt * 128:(tt + 1) * 128, :],
                           stg, sem="st%d" % par)
        st_done[tt] = last_store
        b.wait("tensor", "v", vv)
    b.wait("sync", "st0", b.cnt.get("st0", 0))
    b.wait("sync", "st1", b.cnt.get("st1", 0))


def kernel(**inputs):
    tables = _host_tables()
    nc = build_program()
    feeds = []
    x = np.ascontiguousarray(inputs["L0_attn_hidden"])
    base = {
        "g_wq": inputs["L0_attn_g_wq"], "g_kv": inputs["L0_attn_g_kv_down"],
        "g_ku": inputs["L0_attn_g_k_up"], "g_vu": inputs["L0_attn_g_v_up"],
        "g_wo": inputs["L0_attn_g_wo"], "l_wq": inputs["L0_attn_l_wq"],
        "l_wk": inputs["L0_attn_l_wk"], "l_wv": inputs["L0_attn_l_wv"],
        "l_wo": inputs["L0_attn_l_wo"], "rw": inputs["L0_ffn_router_w"],
        "rb": inputs["L0_ffn_router_b"], "n1w": inputs["L0_attn_norm1_w"],
        "n2w": inputs["L0_attn_norm2_w"], "w1": inputs["L0_ffn_w1"],
        "b1": inputs["L0_ffn_b1"], "w2": inputs["L0_ffn_w2"],
        "b2": inputs["L0_ffn_b2"],
    }
    base = {k: np.ascontiguousarray(v) for k, v in base.items()}
    base.update(tables)
    for c in range(8):
        m = dict(base)
        m["xin"] = x[c]
        feeds.append(m)
    res = run_bass_kernel_spmd(nc, feeds, core_ids=list(range(8)))
    out = np.stack([res.results[c]["yout"] for c in range(8)], axis=0)
    return out.astype(np.float32)


if __name__ == "__main__":
    pass
